# version 06 (final)
# baseline (speedup 1.0000x reference)
import contextlib
import numpy as np
import ml_dtypes
import concourse.bass as bass
import concourse.mybir as mybir
from concourse.bass_utils import run_bass_kernel_spmd

F32 = mybir.dt.float32
BF16 = mybir.dt.bfloat16
I32 = mybir.dt.int32
ALU = mybir.AluOpType
ACT = mybir.ActivationFunctionType
AX = mybir.AxisListType
BF = ml_dtypes.bfloat16

NCORES = 8
B, S, D = 2, 4096, 2048
TOK = (B * S) // NCORES
NT = TOK // 128
NSUP = 2
TSUP = TOK // NSUP
H, HKV, HD = 16, 8, 128
E = 64
DI = 8192
NI = D // 128
NA = DI // 128
INV2PI = 0.15915494309189535
TWOPI = 6.283185307179586


def _hadamard(n):
    h = np.array([[1.0]], dtype=np.float32)
    while h.shape[0] < n:
        h = np.block([[h, h], [h, -h]])
    return h.astype(np.float32)


def _rope_tables(pos):
    inv = (1.0 / (10000.0 ** (np.arange(0, HD, 2, dtype=np.float64) / HD))).astype(np.float32)
    freqs = pos.astype(np.float32)[None, :] * inv[:, None]
    emb = np.concatenate([freqs, freqs], axis=0)
    cos = np.cos(emb.astype(np.float64)).astype(np.float32)
    sin = np.sin(emb.astype(np.float64)).astype(np.float32)
    return cos, sin


def _down_perm():
    perm = np.empty(DI, dtype=np.int64)
    r = 0
    for c in range(64):
        for p_ in range(128):
            beta, ap = p_ // 64, p_ % 64
            perm[r] = ap * 128 + 2 * c + beta
            r += 1
    return perm



class Prog:
    ENGINES = ("tensor", "vector", "scalar", "gpsimd", "sync")

    def __init__(self, nc):
        self.nc = nc
        self.streams = {e: [] for e in self.ENGINES}
        self.sem_handles = {}
        self.counts = {}

    def add_sem(self, name, handle):
        self.sem_handles[name] = handle
        self.counts[name] = 0

    def emit(self, engine, fn, waits=(), inc=None):
        waits = [(s, v) for (s, v) in waits if v > 0]
        if inc is not None:
            self.counts[inc[0]] += inc[1]
        self.streams[engine].append((fn, tuple(waits), inc))
        return self.counts[inc[0]] if inc is not None else None

    def dma(self, queue, out_ap, in_ap, sem, waits=(), transpose=False):
        def fn(eng, out_ap=out_ap, in_ap=in_ap, transpose=transpose):
            return eng.dma_start(out=out_ap, in_=in_ap, transpose=transpose)
        return self.emit(queue, fn, waits=waits, inc=(sem, 16))

    def materialize(self, block):
        for name in self.ENGINES:
            stream = self.streams[name]
            section = getattr(block, name)

            def body(eng, _stream=stream):
                for fn, waits, inc in _stream:
                    for sname, v in waits:
                        eng.wait_ge(self.sem_handles[sname], v)
                    instr = fn(eng)
                    if inc is not None:
                        instr.then_inc(self.sem_handles[inc[0]], inc[1])
            section(body)


SEMS = ["dS", "dT", "dG", "sT", "sV", "sS", "sG"]


A_SEG0 = 0
A_SEG1 = 8192
A_SEG2 = 16384
A_SEG3 = 24576
A_SEG4 = 32768
A_SEG5 = 40960

S1_QSB = 0
S1_SQ = 256
S1_T1T = 512
S1_RREP = 1024
S1_QROPE = 1536
S1_PFT = 2560
S1_FST = 3072
S1_AOT = 3584
S1_RROW = 4608
S1_ROWS = 5632

S5_FS1 = 3072
S5_FS2 = 3584
S5_FS3 = 4096
S5_SHR = 4608


def build_nc():
    nc = bass.Bass("TRN2", target_bir_lowering=False)
    g = {}

    def din(name, shape, dt=F32):
        g[name] = nc.dram_tensor(name, shape, dt, kind="ExternalInput").ap()
    din("x", [TOK, D])
    din("xo", [TOK, D])
    din("wqT", [D, D]); din("wkT", [D, HKV * HD]); din("wvT", [D, HKV * HD])
    din("woT", [D, D])
    din("wgT", [D, DI]); din("wuT", [D, DI]); din("wdS", [DI, D])
    din("wmod", [D, 6 * D])
    din("rT", [HD, E])
    din("cvec", [1, D])
    din("bqc", [128, NI]); din("bkc", [128, HKV]); din("bvc", [128, HKV])
    din("boc", [128, NI])
    din("h128", [128, 128]); din("bd64", [128, 128]); din("pT", [128, 128])
    din("eye", [128, 128])
    din("onesr", [1, 128]); din("onescol", [128, 1])
    din("costab", [128, TOK], BF16); din("sintab", [128, TOK], BF16)
    din("cc", [128, 8])
    g["out"] = nc.dram_tensor("out", [TOK, D], F32, kind="ExternalOutput").ap()
    g["x2park"] = nc.dram_tensor("x2park", [TOK, D], F32, kind="Internal").ap()
    g["zpark"] = nc.dram_tensor("zpark", [NA, 128, TOK], BF16, kind="Internal").ap()
    g["aobuf"] = nc.dram_tensor("aobuf", [TOK, D], F32, kind="Internal").ap()

    p = Prog(nc)
    es = contextlib.ExitStack()
    with es:
        for s in SEMS:
            p.add_sem(s, es.enter_context(nc.semaphore(s)))

        def sb(name, shape, dt):
            g[name] = es.enter_context(nc.sbuf_tensor(name, shape, dt))

        sb("arena", [128, 46080], F32)
        sb("c_h128", [128, 128], BF16)
        sb("c_bd64", [128, 128], BF16)
        sb("c_pT", [128, 128], BF16)
        sb("c_eye", [128, 128], BF16)
        sb("c_eyef", [128, 128], F32)
        sb("c_rT", [HD, E], BF16)
        sb("c_onesr", [1, 128], BF16)
        sb("c_onesrf", [1, 128], F32)
        sb("c_onescf", [128, 1], F32)
        sb("c_onescol", [128, 1], BF16)
        sb("c_cos", [128, TOK], BF16)
        sb("c_sin", [128, TOK], BF16)
        sb("c_cc", [128, 8], F32)
        sb("c_bqc", [128, NI], F32); sb("c_bkc", [128, HKV], F32)
        sb("c_bvc", [128, HKV], F32); sb("c_boc", [128, NI], F32)
        sb("silu_col", [128, 128], BF16)
        sb("mcol", [128, 6 * NI], F32)
        sb("dcol", [128, 8 * NI], F32)
        sb("wsum", [128, 16], F32)
        sb("swcol", [128, 8], F32)
        sb("stat", [128, 6 * NT], F32)
        sb("rows", [1, 928], F32)
        sb("sgrep", [128, D], F32)
        sb("srep2", [128, D], F32)
        for i in range(4):
            g[f"ps{i}"] = es.enter_context(nc.psum_tensor(f"ps{i}", [128, 1024], F32))

        block = es.enter_context(nc.Block())
        emit_program(p, g)
        p.materialize(block)
    return nc


def AR(g, base, n, dt=F32):
    if dt == F32 or dt == I32:
        ap = g["arena"][:, base:base + n]
        return ap if dt == F32 else ap.bitcast(I32)
    return g["arena"][:, base:base + (n + 1) // 2].bitcast(BF16)[:, 0:n]


def ARB(g, base, n):
    return g["arena"][:, base:base + (n + 1) // 2].bitcast(BF16)[:, 0:n]


def bank(g, i):
    return g[f"ps{i // 2}"][:, (i % 2) * 512:(i % 2) * 512 + 512]



def emit_program(p, g):
    emit_prologue(p, g)
    emit_ln1(p, g)
    for sup in range(NSUP):
        emit_attention_super(p, g, sup)
    emit_gather(p, g)
    emit_wo_residual(p, g)
    emit_pass1_means(p, g)
    emit_gateup(p, g)
    emit_fwht(p, g)
    emit_down_out(p, g)
    emit_final_wait(p, g)



def emit_prologue(p, g):
    g["c_cvec"] = g["arena"][0:1, A_SEG4 + 4096:A_SEG4 + 4096 + D]
    g["silu_c"] = g["arena"].bitcast(BF16)[0:1, 2 * (A_SEG4 + 6144):2 * (A_SEG4 + 6144) + D]
    for sbn, hbn in [("c_cos", "costab"), ("c_sin", "sintab"), ("c_cc", "cc"),
                     ("c_bqc", "bqc"), ("c_bkc", "bkc"), ("c_bvc", "bvc"),
                     ("c_boc", "boc"), ("c_eyef", "eye"), ("c_onesrf", "onesr"),
                     ("c_onescf", "onescol")]:
        p.dma("sync", g[sbn][:], g[hbn], "dS")
    p.dma("sync", g["c_cvec"], g["cvec"], "dS")
    for sbn, hbn in [("c_h128", "h128"), ("c_bd64", "bd64"), ("c_pT", "pT"),
                     ("c_eye", "eye"), ("c_rT", "rT"), ("c_onesr", "onesr"),
                     ("c_onescol", "onescol")]:
        p.dma("gpsimd", g[sbn][:], g[hbn], "dG")
    cS, cG = p.counts["dS"], p.counts["dG"]

    g["silu_f"] = g["arena"][0:1, A_SEG4 + 6144:A_SEG4 + 6144 + D]
    p.emit("scalar", lambda eng: eng.activation(out=g["silu_f"],
                                                in_=g["c_cvec"], func=ACT.Silu),
           waits=[("dS", cS)], inc=("sS", 1))
    silu_done = p.counts["sS"]
    fscrP = g["arena"][:, A_SEG5:A_SEG5 + 1024]
    p.emit("vector", lambda eng: eng.memset(fscrP[:, 0:128], 0.0), inc=("sV", 1))
    for c in range(NI):
        def mmc(eng, c=c):
            return eng.matmul(out=bank(g, c % 8)[:, 0:1],
                              lhsT=g["silu_f"][:, c * 128:(c + 1) * 128],
                              rhs=g["c_cc"][0:1, 4:5], start=True, stop=True)
        p.emit("tensor", mmc, waits=[("sS", silu_done), ("dG", cG), ("dS", cS)] +
               ([("sV", p.counts["sV"])] if c >= 8 else []),
               inc=("sT", 1))
        tn = p.counts["sT"]
        p.emit("vector", lambda eng, c=c: eng.tensor_copy(
            out=fscrP[:, 8 * c:8 * c + 1], in_=bank(g, c % 8)[:, 0:1]),
            waits=[("sT", tn)], inc=("sV", 1))
    p.emit("vector", lambda eng: eng.tensor_copy(
        out=g["silu_col"][:], in_=fscrP[:, 0:128]), inc=("sV", 1))
    scol_done = p.counts["sV"]
    g["scol_sT"] = p.counts["sT"]

    ringb = [ARB(g, A_SEG3, 4096), ARB(g, A_SEG3 + 2048, 4096)]
    mm_after = {}
    rowscr = g["rows"][0:1, 0:512]
    for og in range(3):
        for i in range(NI):
            si = og * NI + i
            slab = ringb[si % 2]
            w = p.dma("gpsimd", slab,
                      g["wmod"][i * 128:(i + 1) * 128, og * 4096:(og + 1) * 4096],
                      "dG",
                      waits=[("sT", mm_after[si - 2])] if si >= 2 else ())
            for j in range(8):
                def mm(eng, i=i, j=j, slab=slab):
                    return eng.matmul(out=bank(g, j)[0:1, :],
                                      lhsT=g["silu_col"][:, 8 * i:8 * i + 1],
                                      rhs=slab[:, j * 512:(j + 1) * 512],
                                      start=(i == 0), stop=(i == NI - 1))
                p.emit("tensor", mm,
                       waits=([("dG", w), ("sV", scol_done)] if j == 0 else []) +
                             ([("sV", p.counts["sV"])] if (j == 0 and i == 0 and og > 0) else []),
                       inc=("sT", 1))
            mm_after[si] = p.counts["sT"]
        ogd = p.counts["sT"]
        for j in (7, 0, 1, 2, 3, 4, 5, 6):
            def evr(eng, j=j):
                return eng.tensor_copy(out=rowscr, in_=bank(g, j)[0:1, :])
            p.emit("vector", evr, waits=[("sT", ogd)], inc=("sV", 1))
            evd = p.counts["sV"]
            def evb(eng):
                return eng.tensor_copy(
                    out=g["rows"][0:1, 512:768].bitcast(BF16),
                    in_=rowscr)
            p.emit("vector", evb, inc=("sV", 1))
            evbd = p.counts["sV"]
            for q in range(4):
                cidx = og * 32 + j * 4 + q
                def cmm(eng, q=q):
                    src = g["rows"][0:1, 512:768].bitcast(BF16)[0:1, q * 128:q * 128 + 128]
                    return eng.matmul(out=g["ps3"][:, 512 + 2 * q:512 + 2 * q + 1],
                                      lhsT=src, rhs=g["c_onesr"][0:1, 0:1],
                                      start=True, stop=True)
                p.emit("tensor", cmm, waits=[("sV", evbd)] if q == 0 else (), inc=("sT", 1))
                tn = p.counts["sT"]
                def cev(eng, cidx=cidx, q=q):
                    return eng.tensor_copy(out=g["mcol"][:, cidx:cidx + 1],
                                           in_=g["ps3"][:, 512 + 2 * q:512 + 2 * q + 1])
                p.emit("vector", cev, waits=[("sT", tn)], inc=("sV", 1))
    mcol_done = p.counts["sV"]

    M, Dc = g["mcol"], g["dcol"]
    def dv(fn, waits=()):
        p.emit("vector", fn, waits=waits, inc=("sV", 1))
    dv(lambda eng: eng.tensor_scalar(out=Dc[:, 0:NI], in0=M[:, NI:2 * NI],
                                     scalar1=1.0, scalar2=None, op0=ALU.add),
       waits=[("sV", mcol_done)])
    dv(lambda eng: eng.tensor_copy(out=Dc[:, NI:2 * NI], in_=M[:, 0:NI]))
    dv(lambda eng: eng.tensor_scalar(out=Dc[:, 2 * NI:3 * NI], in0=M[:, 4 * NI:5 * NI],
                                     scalar1=1.0, scalar2=None, op0=ALU.add))
    dv(lambda eng: eng.tensor_copy(out=Dc[:, 3 * NI:4 * NI], in_=M[:, 3 * NI:4 * NI]))
    for src_slot, dst_slot in ((2, 4), (5, 6)):
        p.emit("scalar", lambda eng, s=src_slot: eng.activation(
            out=g["wsum"][:, 0:NI], in_=M[:, s * NI:(s + 1) * NI], func=ACT.Abs),
            waits=[("sV", p.counts["sV"])], inc=("sS", 1))
        p.emit("scalar", lambda eng, s=src_slot, d=dst_slot: eng.activation(
            out=Dc[:, d * NI:(d + 1) * NI], in_=M[:, s * NI:(s + 1) * NI],
            func=ACT.Sign), inc=("sS", 1))
        ad = p.counts["sS"]
        dv(lambda eng: eng.tensor_scalar(
            out=g["dcol"][:, 112:112 + NI], in0=g["wsum"][:, 0:NI],
            scalar1=0.25, scalar2=None, op0=ALU.max), waits=[("sS", ad)])
        emit_gap(p, g)
        dv(lambda eng, d=dst_slot: eng.tensor_tensor(
            out=g["wsum"][:, 0:NI], in0=Dc[:, d * NI:(d + 1) * NI],
            in1=g["dcol"][:, 112:112 + NI], op=ALU.mult))
        emit_gap(p, g)
        dv(lambda eng, d=dst_slot: eng.tensor_copy(
            out=Dc[:, d * NI:(d + 1) * NI], in_=g["wsum"][:, 0:NI]))
    dv(lambda eng: eng.tensor_tensor(out=Dc[:, 5 * NI:6 * NI],
                                     in0=Dc[:, 4 * NI:5 * NI],
                                     in1=g["c_boc"][:], op=ALU.mult))
    g["dcol_done"] = p.counts["sV"]

    rowf = g["rows"][0:1, 768:896]
    g["ca1rep"] = ARB(g, A_SEG3, 2048)
    g["sarep"] = ARB(g, A_SEG3 + 1024, 2048)
    for c in range(NI):
        for slot, dst, addone in ((4, "sgrep", True), (3, "srep2", False),
                                  (1, "ca1rep", True), (0, "sarep", False)):
            def colb(eng, slot=slot, c=c):
                return eng.tensor_copy(out=g["wsum"][:, 12:13],
                                       in_=M[:, slot * NI + c:slot * NI + c + 1])
            p.emit("vector", colb, waits=[("sV", g["dcol_done"])] if c == 0 else
                   [("sT", p.counts["sT"])], inc=("sV", 1))
            cbd = p.counts["sV"]
            def c2r(eng):
                return eng.matmul(out=g["ps2"][0:1, 512:512 + 128],
                                  lhsT=g["wsum"][:, 12:13],
                                  rhs=g["c_eyef"][:], start=True, stop=True)
            p.emit("tensor", c2r, waits=[("sV", cbd)], inc=("sT", 1))
            td = p.counts["sT"]
            def rowb(eng, addone=addone, rowf=rowf):
                return eng.tensor_scalar(out=rowf, in0=g["ps2"][0:1, 512:512 + 128],
                                         scalar1=(1.0 if addone else 0.0),
                                         scalar2=None, op0=ALU.add)
            p.emit("vector", rowb, waits=[("sT", td)], inc=("sV", 1))
            rbd = p.counts["sV"]
            def rep(eng, rowf=rowf):
                return eng.matmul(out=g["ps3"][:, 0:128], lhsT=g["c_onesrf"][:],
                                  rhs=rowf, start=True, stop=True)
            p.emit("tensor", rep, waits=[("sV", rbd)], inc=("sT", 1))
            td2 = p.counts["sT"]
            def repe(eng, dst=dst, c=c):
                tgt = g[dst]
                tgt = tgt[:, c * 128:(c + 1) * 128] if dst in ("sgrep", "srep2") \
                    else tgt[:, c * 128:(c + 1) * 128]
                return eng.tensor_copy(out=tgt, in_=g["ps3"][:, 0:128])
            p.emit("vector", repe, waits=[("sT", td2)], inc=("sV", 1))
    g["cbsb_done"] = p.counts["sV"]



def emit_ln1(p, g):
    xslot = AR(g, A_SEG4 + 4096, 2048)
    sqslot = AR(g, A_SEG4 + 6144, 2048)
    x2slot = AR(g, A_SEG4 + 0, 2048)
    fscr = AR(g, A_SEG5, 1024)
    for t in range(NT):
        w = p.dma("sync", xslot, g["x"][t * 128:(t + 1) * 128, :], "dS",
                  waits=[("dT", p.counts["dT"]), ("sS", p.counts["sS"])] if t > 0 else
                        [("sT", g["scol_sT"]), ("sS", p.counts["sS"])])
        def red(eng, t=t):
            return eng.tensor_reduce(out=fscr[:, 1:2],
                                     in_=xslot, axis=AX.X, op=ALU.add)
        p.emit("vector", red, waits=[("dS", w)], inc=("sV", 1))
        def sq(eng, t=t):
            return eng.activation(out=sqslot, in_=xslot, func=ACT.Square)
        p.emit("scalar", sq, waits=[("dS", w)], inc=("sS", 1))
        sqd = p.counts["sS"]
        def sqr(eng, t=t):
            return eng.tensor_reduce(out=fscr[:, 2:3],
                                     in_=sqslot, axis=AX.X, op=ALU.add)
        p.emit("vector", sqr, waits=[("sS", sqd)], inc=("sV", 1))
        emit_stats_tail(p, g, 0 * NT + t, 1 * NT + t, sqd, fscr)
        def t1(eng, t=t):
            return eng.tensor_scalar(out=ARB(g, A_SEG1 + t * 1024, 2048), in0=xslot,
                                     scalar1=g["stat"][:, 0 * NT + t:0 * NT + t + 1],
                                     scalar2=g["stat"][:, 1 * NT + t:1 * NT + t + 1],
                                     op0=ALU.subtract, op1=ALU.mult)
        p.emit("vector", t1, waits=[("sV", g["cbsb_done"])] if t == 0 else (),
               inc=("sV", 1))
        def ha(eng, t=t):
            return eng.tensor_tensor(out=x2slot, in0=ARB(g, A_SEG1 + t * 1024, 2048),
                                     in1=g["ca1rep"], op=ALU.mult)
        p.emit("vector", ha, inc=("sV", 1))
        def hb(eng, t=t):
            return eng.tensor_tensor(out=ARB(g, A_SEG1 + t * 1024, 2048), in0=x2slot,
                                     in1=g["sarep"], op=ALU.add)
        p.emit("vector", hb, inc=("sV", 1))
        vt1 = p.counts["sV"]
        for c in range(NI):
            p.dma("scalar",
                  ARB(g, A_SEG0, NI * TOK)[:, c * TOK + t * 128: c * TOK + t * 128 + 128],
                  ARB(g, A_SEG1 + t * 1024, 2048)[:, c * 128:(c + 1) * 128],
                  "dT", waits=[("sV", vt1)] if c == 0 else (), transpose=True)
    g["h_done"] = p.counts["sV"]
    g["h_dT"] = p.counts["dT"]


def emit_gap(p, g):
    p.emit("vector", lambda eng: eng.tensor_copy(out=g["swcol"][:, 7:8],
                                                 in_=g["c_cc"][:, 0:1]),
           inc=("sV", 1))


def emit_stats_tail(p, g, mslot, vslot, sq_done, fscr):
    def mu(eng):
        return eng.tensor_scalar(out=g["stat"][:, mslot:mslot + 1],
                                 in0=fscr[:, 1:2],
                                 scalar1=1.0 / D, scalar2=None, op0=ALU.mult)
    p.emit("vector", mu, inc=("sV", 1))
    def var(eng):
        return eng.tensor_scalar(out=fscr[:, 0:1], in0=fscr[:, 2:3],
                                 scalar1=1.0 / D, scalar2=None, op0=ALU.mult)
    p.emit("vector", var, waits=[("sS", sq_done)], inc=("sV", 1))
    def mu2(eng):
        return eng.tensor_tensor(out=fscr[:, 3:4], in0=g["stat"][:, mslot:mslot + 1],
                                 in1=g["stat"][:, mslot:mslot + 1], op=ALU.mult)
    p.emit("vector", mu2, inc=("sV", 1))
    emit_gap(p, g)
    def var2(eng):
        return eng.tensor_tensor(out=g["stat"][:, vslot:vslot + 1],
                                 in0=fscr[:, 0:1], in1=fscr[:, 3:4], op=ALU.subtract)
    p.emit("vector", var2, inc=("sV", 1))
    vd = p.counts["sV"]
    def rho1(eng):
        return eng.activation(out=fscr[:, 0:1],
                              in_=g["stat"][:, vslot:vslot + 1], func=ACT.Sqrt,
                              bias=g["c_cc"][:, 2:3])
    p.emit("scalar", rho1, waits=[("sV", vd)], inc=("sS", 1))
    rd = p.counts["sS"]
    def rho2(eng):
        return eng.reciprocal(out=g["stat"][:, vslot:vslot + 1],
                              in_=fscr[:, 0:1])
    p.emit("vector", rho2, waits=[("sS", rd)], inc=("sV", 1))
    emit_gap(p, g)



def emit_attention_super(p, g, sup):
    t0 = sup * TSUP
    hfm3 = ARB(g, A_SEG0, NI * TOK).rearrange("p (c t) -> p c t", c=NI)
    wring = ARB(g, A_SEG4, 1024)
    coss = g["c_cos"][:, t0:t0 + TSUP]
    sins = g["c_sin"][:, t0:t0 + TSUP]
    q_sb = ARB(g, A_SEG1 + S1_QSB, 512)
    sq_sc = ARB(g, A_SEG1 + S1_SQ, 512)
    t1t = AR(g, A_SEG1 + S1_T1T, 512)
    rrep = AR(g, A_SEG1 + S1_RREP, 512)
    pft = AR(g, A_SEG1 + S1_PFT, 512)
    fs1 = AR(g, A_SEG5 + S5_FS1, 512)
    fs2 = AR(g, A_SEG5 + S5_FS2, 512)
    fs3 = AR(g, A_SEG5 + S5_FS3, 512)
    fi32 = AR(g, A_SEG5 + S5_SHR, 512, I32)
    sinq_t = ARB(g, A_SEG2, 4 * H * E)
    cosq_t = ARB(g, A_SEG2 + 2048, 4 * H * E)
    sink_t = ARB(g, A_SEG2 + 4096, 4 * HKV * E)
    cosk_t = ARB(g, A_SEG2 + 5120, 4 * HKV * E)
    v_t = ARB(g, A_SEG2 + 6144, 4 * HKV * HD)
    chunks = [("q", i) for i in range(NI)] + [("k", i) for i in range(HKV)] + \
             [("v", i) for i in range(HKV)]
    wmap = {"q": ("wqT", "c_bqc"), "k": ("wkT", "c_bkc"), "v": ("wvT", "c_bvc")}
    mm_of_load = {}
    g.setdefault("feat_sS", 0)
    for ci, (kind, oc) in enumerate(chunks):
        wname, bcol = wmap[kind]
        pb = bank(g, ci % 4)
        for i in range(NI):
            li = ci * NI + i
            slot = wring[:, (li % 8) * 128:(li % 8) * 128 + 128]
            w = p.dma("gpsimd", slot,
                      g[wname][i * 128:(i + 1) * 128, oc * 128:(oc + 1) * 128],
                      "dG", waits=[("sT", mm_of_load[(sup, li - 8)])] if li >= 8 else
                            [("sV", g["h_done"])])
            wt = [("dG", w)]
            if i == 0:
                wt.append(("sV", p.counts["sV"]))
                if sup == 0 and ci == 0:
                    wt.append(("dT", g["h_dT"]))
            def mm(eng, slot=slot, i=i, pb=pb):
                return eng.matmul(out=pb[:, 0:TSUP], lhsT=slot,
                                  rhs=hfm3[:, i, t0:t0 + TSUP],
                                  start=(i == 0), stop=(i == NI - 1))
            p.emit("tensor", mm, waits=wt, inc=("sT", 1))
            mm_of_load[(sup, li)] = p.counts["sT"]
        qkv_done = p.counts["sT"]
        if kind == "v":
            def ev(eng, pb=pb, oc=oc):
                return eng.tensor_scalar(out=ARB(g, A_SEG1 + S1_FST, 512), in0=pb[:, 0:TSUP],
                                         scalar1=g["c_bvc"][:, oc:oc + 1], scalar2=None,
                                         op0=ALU.add)
            p.emit("vector", ev, waits=[("sT", qkv_done), ("dT", p.counts["dT"])],
                   inc=("sV", 1))
            vv = p.counts["sV"]
            for b4 in range(4):
                p.dma("scalar",
                      v_t[:, b4 * HKV * HD + oc * HD: b4 * HKV * HD + oc * HD + 128],
                      ARB(g, A_SEG1 + S1_FST, 512)[:, b4 * 128:(b4 + 1) * 128],
                      "dT", waits=[("sV", vv)], transpose=True)
            continue
        def ev(eng, pb=pb, oc=oc, bcol=bcol):
            return eng.tensor_scalar(out=q_sb, in0=pb[:, 0:TSUP],
                                     scalar1=g[bcol][:, oc:oc + 1], scalar2=None,
                                     op0=ALU.add)
        p.emit("vector", ev, waits=[("sT", qkv_done), ("sS", g["feat_sS"])],
               inc=("sV", 1))
        evd = p.counts["sV"]
        def prot(eng):
            return eng.matmul(out=bank(g, 4)[:, 0:TSUP], lhsT=g["c_pT"][:],
                              rhs=q_sb, start=True, stop=True)
        p.emit("tensor", prot, waits=[("sV", evd)], inc=("sT", 1))
        protd = p.counts["sT"]
        def sqr(eng):
            return eng.activation(out=sq_sc, in_=q_sb, func=ACT.Square)
        p.emit("scalar", sqr, waits=[("sV", evd)], inc=("sS", 1))
        sqd = p.counts["sS"]
        def r1(eng):
            return eng.tensor_tensor(out=t1t, in0=q_sb, in1=coss, op=ALU.mult)
        p.emit("vector", r1, inc=("sV", 1))
        def r2(eng):
            return eng.tensor_tensor(out=fs1, in0=bank(g, 4)[:, 0:TSUP], in1=sins,
                                     op=ALU.mult)
        p.emit("vector", r2, waits=[("sT", protd)], inc=("sV", 1))
        hslot = oc if kind == "q" else H + oc
        qr = ARB(g, A_SEG1 + S1_QROPE + (hslot % 4) * 256, 512)
        def r3(eng, qr=qr):
            return eng.tensor_tensor(out=qr, in0=t1t, in1=fs1, op=ALU.add)
        p.emit("vector", r3, inc=("sV", 1))
        g[f"rope_{hslot}"] = p.counts["sV"]
        def ssq(eng):
            return eng.matmul(out=bank(g, 5)[0:1, 0:TSUP], lhsT=g["c_onescol"][:],
                              rhs=sq_sc, start=True, stop=True)
        p.emit("tensor", ssq, waits=[("sS", sqd), ("sV", p.counts["sV"])], inc=("sT", 1))
        ssqd = p.counts["sT"]
        def rr1(eng):
            return eng.activation(out=fs2[0:1, :], in_=bank(g, 5)[0:1, 0:TSUP],
                                  func=ACT.Sqrt)
        p.emit("scalar", rr1, waits=[("sT", ssqd)], inc=("sS", 1))
        g["feat_sS"] = p.counts["sS"]
        rrd = p.counts["sS"]
        def rr2(eng):
            return eng.tensor_scalar(out=fs3[0:1, :], in0=fs2[0:1, :],
                                     scalar1=1e-6, scalar2=None, op0=ALU.max)
        p.emit("vector", rr2, waits=[("sS", rrd)], inc=("sV", 1))
        def rr3(eng):
            return eng.reciprocal(out=fs2[0:1, :], in_=fs3[0:1, :])
        p.emit("vector", rr3, inc=("sV", 1))
        rrowb = g["arena"].bitcast(BF16)[0:1, 2 * (A_SEG1 + S1_RROW) + (hslot % 4) * 512:
                                         2 * (A_SEG1 + S1_RROW) + (hslot % 4) * 512 + 512]
        def rr4(eng, rrowb=rrowb):
            return eng.tensor_copy(out=rrowb, in_=fs2[0:1, :])
        p.emit("vector", rr4, inc=("sV", 1))
        g[f"rrow_{hslot}"] = p.counts["sV"]
        if hslot % 2 == 1:
            emit_feat_pair(p, g, sup, hslot - 1, sinq_t, cosq_t, sink_t, cosk_t,
                           pft, fs1, fs2, fs3, fi32, rrep)
    emit_mixer(p, g, sup, sinq_t, cosq_t, sink_t, cosk_t, v_t)


def emit_feat_pair(p, g, sup, h0, sinq_t, cosq_t, sink_t, cosk_t,
                   pft, fs1, fs2, fs3, fi32, rrep):
    h1 = h0 + 1
    qr0 = ARB(g, A_SEG1 + S1_QROPE + (h0 % 4) * 256, 512)
    qr1 = ARB(g, A_SEG1 + S1_QROPE + (h1 % 4) * 256, 512)
    w0 = [("sV", max(g[f"rope_{h0}"], g[f"rope_{h1}"], g[f"rrow_{h1}"])),
          ("sV", g.get("pair_consumed", 0))]
    def pm0(eng, qr0=qr0):
        return eng.matmul(out=bank(g, 6)[0:64, :], lhsT=g["c_rT"][:], rhs=qr0,
                          start=True, stop=True)
    p.emit("tensor", pm0, waits=w0, inc=("sT", 1))
    def pm1(eng, qr1=qr1):
        return eng.matmul(out=bank(g, 6)[64:128, :], lhsT=g["c_rT"][:], rhs=qr1,
                          start=True, stop=True, tile_position=(0, 64))
    p.emit("tensor", pm1, inc=("sT", 1))
    pmd = p.counts["sT"]
    rb0 = g["arena"].bitcast(BF16)[0:1, 2 * (A_SEG1 + S1_RROW) + (h0 % 4) * 512:
                                   2 * (A_SEG1 + S1_RROW) + (h0 % 4) * 512 + 512]
    rb1 = g["arena"].bitcast(BF16)[0:1, 2 * (A_SEG1 + S1_RROW) + (h1 % 4) * 512:
                                   2 * (A_SEG1 + S1_RROW) + (h1 % 4) * 512 + 512]
    def rm0(eng, rb0=rb0):
        return eng.matmul(out=bank(g, 7)[0:64, :], lhsT=g["c_onesr"][0:1, 0:64],
                          rhs=rb0, start=True, stop=True)
    p.emit("tensor", rm0, inc=("sT", 1))
    def rm1(eng, rb1=rb1):
        return eng.matmul(out=bank(g, 7)[64:128, :], lhsT=g["c_onesr"][0:1, 0:64],
                          rhs=rb1, start=True, stop=True, tile_position=(0, 64))
    p.emit("tensor", rm1, inc=("sT", 1))
    rmd = p.counts["sT"]
    def rev(eng):
        return eng.tensor_copy(out=rrep, in_=bank(g, 7)[:, 0:TSUP])
    p.emit("vector", rev, waits=[("sT", rmd)], inc=("sV", 1))
    def pmul(eng):
        return eng.tensor_tensor(out=pft, in0=bank(g, 6)[:, 0:TSUP], in1=rrep,
                                 op=ALU.mult)
    p.emit("vector", pmul, waits=[("sT", pmd), ("sS", g["feat_sS"])], inc=("sV", 1))
    g["pair_consumed"] = p.counts["sV"]
    def u1(eng):
        return eng.tensor_scalar(out=fs1, in0=pft, scalar1=INV2PI, scalar2=None,
                                 op0=ALU.mult)
    p.emit("vector", u1, inc=("sV", 1))
    def u2(eng):
        return eng.tensor_copy(out=fi32, in_=fs1)
    p.emit("vector", u2, inc=("sV", 1))
    def u3(eng):
        return eng.tensor_tensor(out=fs2, in0=fs1, in1=fi32, op=ALU.subtract)
    p.emit("vector", u3, inc=("sV", 1))
    def u4(eng):
        return eng.tensor_scalar(out=fs3, in0=fs2, scalar1=0.5, scalar2=None,
                                 op0=ALU.is_ge)
    p.emit("vector", u4, inc=("sV", 1))
    def u5(eng):
        return eng.tensor_tensor(out=fs1, in0=fs2, in1=fs3, op=ALU.subtract)
    p.emit("vector", u5, inc=("sV", 1))
    def u6(eng):
        return eng.tensor_scalar(out=fs3, in0=fs2, scalar1=-0.5, scalar2=None,
                                 op0=ALU.is_le)
    p.emit("vector", u6, inc=("sV", 1))
    def u7(eng):
        return eng.tensor_tensor(out=fs2, in0=fs1, in1=fs3, op=ALU.add)
    p.emit("vector", u7, inc=("sV", 1))
    usd = p.counts["sV"]
    def au(eng):
        return eng.activation(out=fs1, in_=fs2, func=ACT.Abs)
    p.emit("scalar", au, waits=[("sV", usd)], inc=("sS", 1))
    def sgn(eng):
        return eng.activation(out=fs3, in_=fs2, func=ACT.Sign)
    p.emit("scalar", sgn, inc=("sS", 1))
    aud = p.counts["sS"]
    def w1(eng):
        return eng.tensor_scalar(out=pft, in0=fs1, scalar1=-1.0, scalar2=0.5,
                                 op0=ALU.mult, op1=ALU.add)
    p.emit("vector", w1, waits=[("sS", aud)], inc=("sV", 1))
    def w2(eng):
        return eng.tensor_tensor(out=fs2, in0=pft, in1=fs1, op=ALU.min)
    p.emit("vector", w2, inc=("sV", 1))
    def w3(eng):
        return eng.tensor_tensor(out=pft, in0=fs2, in1=fs3, op=ALU.mult)
    p.emit("vector", w3, inc=("sV", 1))
    wsd = p.counts["sV"]
    sstage = ARB(g, A_SEG1 + S1_FST, 512)
    cstage = ARB(g, A_SEG1 + S1_FST + 256, 512)
    def s1(eng):
        return eng.activation(out=sstage, in_=pft, func=ACT.Sin, scale=TWOPI)
    p.emit("scalar", s1, waits=[("sV", wsd), ("dT", p.counts["dT"])], inc=("sS", 1))
    def wc(eng):
        return eng.tensor_scalar(out=fs2, in0=fs1, scalar1=-1.0, scalar2=0.25,
                                 op0=ALU.mult, op1=ALU.add)
    p.emit("vector", wc, inc=("sV", 1))
    wcd = p.counts["sV"]
    def s2(eng):
        return eng.activation(out=cstage, in_=fs2, func=ACT.Sin, scale=TWOPI)
    p.emit("scalar", s2, waits=[("sV", wcd)], inc=("sS", 1))
    g["feat_sS"] = p.counts["sS"]
    fd = p.counts["sS"]
    pr = h0 // 2
    if h0 < H:
        for b4 in range(4):
            p.dma("scalar", sinq_t[:, b4 * H * E + pr * 128: b4 * H * E + pr * 128 + 128],
                  sstage[:, b4 * 128:(b4 + 1) * 128], "dT",
                  waits=[("sS", fd)] if b4 == 0 else (), transpose=True)
            p.dma("scalar", cosq_t[:, b4 * H * E + pr * 128: b4 * H * E + pr * 128 + 128],
                  cstage[:, b4 * 128:(b4 + 1) * 128], "dT", transpose=True)
    else:
        prk = (h0 - H) // 2
        for b4 in range(4):
            p.dma("scalar", sink_t[:, b4 * HKV * E + prk * 128: b4 * HKV * E + prk * 128 + 128],
                  sstage[:, b4 * 128:(b4 + 1) * 128], "dT",
                  waits=[("sS", fd)] if b4 == 0 else (), transpose=True)
            p.dma("scalar", cosk_t[:, b4 * HKV * E + prk * 128: b4 * HKV * E + prk * 128 + 128],
                  cstage[:, b4 * 128:(b4 + 1) * 128], "dT", transpose=True)
    g["pair_dT"] = p.counts["dT"]


def emit_mixer(p, g, sup, sinq_t, cosq_t, sink_t, cosk_t, v_t):
    ftd = p.counts["dT"]
    mprod = AR(g, A_SEG4 + 1024, 2048)
    accA = AR(g, A_SEG4 + 4096, 2048)
    accB = AR(g, A_SEG4 + 6144, 2048)
    for b4 in range(4):
        tt = sup * 4 + b4
        sq3 = sinq_t[:, b4 * H * E:(b4 + 1) * H * E].rearrange("p (h e) -> p h e", h=H)
        cq3 = cosq_t[:, b4 * H * E:(b4 + 1) * H * E].rearrange("p (h e) -> p h e", h=H)
        sk3 = sink_t[:, b4 * HKV * E:(b4 + 1) * HKV * E].rearrange("p (j e) -> p j e", j=HKV)
        ck3 = cosk_t[:, b4 * HKV * E:(b4 + 1) * HKV * E].rearrange("p (j e) -> p j e", j=HKV)
        pr3 = mprod[:].rearrange("p (h f) -> p h f", h=H)
        cm = AR(g, A_SEG1 + S1_AOT, 128)
        cm3 = cm.rearrange("p (h j) -> p h j", j=HKV)
        for j in range(HKV):
            skb = sk3[:, j:j + 1, :].broadcast_to([128, H, E])
            ckb = ck3[:, j:j + 1, :].broadcast_to([128, H, E])
            def c1(eng, sq3=sq3, skb=skb, pr3=pr3):
                return eng.tensor_tensor(out=pr3[:, :, 0:E], in0=sq3, in1=skb, op=ALU.mult)
            p.emit("vector", c1, waits=[("dT", ftd)] if (b4 == 0 and j == 0) else (),
                   inc=("sV", 1))
            def c2(eng, cq3=cq3, ckb=ckb, pr3=pr3):
                return eng.tensor_tensor(out=pr3[:, :, E:2 * E], in0=cq3, in1=ckb, op=ALU.mult)
            p.emit("vector", c2, inc=("sV", 1))
            def c3r(eng, pr3=pr3, cm3=cm3, j=j):
                return eng.tensor_reduce(out=cm3[:, :, j], in_=pr3, axis=AX.X, op=ALU.add)
            p.emit("vector", c3r, inc=("sV", 1))
        cm2 = AR(g, A_SEG1 + S1_AOT + 128, 128)
        emit_gap(p, g)
        def cs(eng, cm=cm, cm2=cm2):
            return eng.tensor_scalar(out=cm2, in0=cm, scalar1=1.0 / 64.0, scalar2=None,
                                     op0=ALU.mult)
        p.emit("vector", cs, inc=("sV", 1))
        cm3s = cm2.rearrange("p (h j) -> p h j", j=HKV)
        v3 = v_t[:, b4 * HKV * HD:(b4 + 1) * HKV * HD].rearrange("p (j d) -> p j d", j=HKV)
        accs = [accA, accB]
        for j in range(HKV):
            vb = v3[:, j:j + 1, :].broadcast_to([128, H, HD])
            cb = cm3s[:, :, j:j + 1].broadcast_to([128, H, HD])
            if j == 0:
                def m0(eng, vb=vb, cb=cb):
                    return eng.tensor_tensor(out=accA.rearrange("p (h d) -> p h d", h=H),
                                             in0=vb, in1=cb, op=ALU.mult)
                p.emit("vector", m0, waits=[("dS", g.get("macc_busy", 0))],
                       inc=("sV", 1))
            else:
                def m1(eng, vb=vb, cb=cb):
                    return eng.tensor_tensor(out=mprod.rearrange("p (h d) -> p h d", h=H),
                                             in0=vb, in1=cb, op=ALU.mult)
                p.emit("vector", m1, inc=("sV", 1))
                def m2(eng, j=j):
                    return eng.tensor_tensor(out=accs[j % 2], in0=accs[(j + 1) % 2],
                                             in1=mprod, op=ALU.add)
                p.emit("vector", m2, inc=("sV", 1))
        aod = p.counts["sV"]
        w_ao = p.dma("sync", g["aobuf"][tt * 128:(tt + 1) * 128, :], accs[(HKV - 1) % 2],
                     "dS", waits=[("sV", aod)])
        g["ao_dS"] = w_ao
        g["macc_busy"] = w_ao
    g[f"attn_dT_{sup}"] = p.counts["dT"]



def emit_gather(p, g):
    src5 = g["aobuf"].rearrange("(b e j) (g d) -> g b e j d", b=4, e=16, j=16, g=16)
    gt = AR(g, A_SEG4 + 4096, 2048)
    for Tt in range(NT):
        w = p.dma("sync", gt, src5[2 * Tt:2 * Tt + 2],
                  "dS", waits=[("sV", p.counts["sV"])] if Tt > 0 else ())
        stg = ARB(g, A_SEG1 + S1_AOT, 2048)
        def cst(eng, stg=stg):
            return eng.tensor_copy(out=stg, in_=gt)
        p.emit("vector", cst, waits=[("dS", w)], inc=("sV", 1))
        cd = p.counts["sV"]
        aof = ARB(g, A_SEG3, NI * TOK)
        for c in range(NI):
            p.dma("scalar",
                  aof[:, c * TOK + Tt * 128: c * TOK + Tt * 128 + 128],
                  stg[:, c * 128:(c + 1) * 128],
                  "dT", waits=[("sV", cd)] if c == 0 else (), transpose=True)
    g["gather_dT"] = p.counts["dT"]


def emit_wo_residual(p, g):
    aod = g["gather_dT"]
    aof3 = ARB(g, A_SEG3, NI * TOK).rearrange("p (c t) -> p c t", c=NI)
    wring = ARB(g, A_SEG4, 1024)
    r1f = ARB(g, A_SEG0, 1024)
    mm_of_load = {}
    for oc in range(NI):
        pq = g[f"ps{oc % 4}"]
        for i in range(NI):
            li = oc * NI + i
            slot = wring[:, (li % 8) * 128:(li % 8) * 128 + 128]
            w = p.dma("gpsimd", slot,
                      g["woT"][i * 128:(i + 1) * 128, oc * 128:(oc + 1) * 128],
                      "dG", waits=[("sT", mm_of_load[li - 8])] if li >= 8 else ())
            wt = [("dG", w)]
            if i == 0:
                wt.append(("sV", p.counts["sV"]))
                if oc == 0:
                    wt.append(("dT", aod))
            for half in range(2):
                def mm(eng, slot=slot, i=i, pq=pq, half=half):
                    return eng.matmul(out=pq[:, half * 512:half * 512 + 512],
                                      lhsT=slot,
                                      rhs=aof3[:, i, half * 512:half * 512 + 512],
                                      start=(i == 0), stop=(i == NI - 1))
                p.emit("tensor", mm, waits=wt if half == 0 else (), inc=("sT", 1))
            mm_of_load[li] = p.counts["sT"]
        mmd = p.counts["sT"]
        for half in range(2):
            def ev(eng, pq=pq, oc=oc, half=half):
                return eng.tensor_scalar(out=r1f[:, half * 512:half * 512 + 512],
                                         in0=pq[:, half * 512:half * 512 + 512],
                                         scalar1=g["dcol"][:, 4 * NI + oc:4 * NI + oc + 1],
                                         scalar2=g["dcol"][:, 5 * NI + oc:5 * NI + oc + 1],
                                         op0=ALU.mult, op1=ALU.add)
            p.emit("vector", ev, waits=[("sT", mmd), ("dT", p.counts["dT"])] if half == 0
                   else [("dT", p.counts["dT"])], inc=("sV", 1))
            evd = p.counts["sV"]
            for b in range(4):
                tt = half * 4 + b
                p.dma("scalar",
                      ARB(g, A_SEG1 + tt * 1024, 2048)[:, oc * 128:oc * 128 + 128],
                      r1f[:, half * 512 + b * 128: half * 512 + b * 128 + 128],
                      "dT", waits=[("sV", evd)] if b == 0 else (), transpose=True)
    r1d = p.counts["dT"]
    xslot = AR(g, A_SEG4 + 4096, 2048)
    x2slot = AR(g, A_SEG4 + 6144, 2048)
    fscr = AR(g, A_SEG5, 1024)
    aq1 = AR(g, A_SEG5 + S5_FS1, 512)
    aq2 = AR(g, A_SEG5 + S5_FS2, 512, I32)
    aq3 = AR(g, A_SEG5 + S5_FS3, 512)
    for t in range(NT):
        w = p.dma("sync", xslot, g["xo"][t * 128:(t + 1) * 128, :], "dS",
                  waits=[("sV", p.counts["sV"]), ("sS", p.counts["sS"])] if t > 0
                        else [("dT", r1d)])
        def res(eng, t=t):
            return eng.tensor_tensor(out=x2slot, in0=xslot,
                                     in1=ARB(g, A_SEG1 + t * 1024, 2048), op=ALU.add)
        p.emit("vector", res, waits=[("dS", w)] + ([("dT", r1d)] if t == 0 else []) +
               ([("sS", p.counts["sS"])] if t > 0 else []),
               inc=("sV", 1))
        x2d = p.counts["sV"]
        wpark = p.dma("sync", g["x2park"][t * 128:(t + 1) * 128, :], x2slot, "dS",
                      waits=[("sV", x2d)])
        def red(eng, t=t):
            return eng.tensor_reduce(out=fscr[:, 1:2],
                                     in_=x2slot, axis=AX.X, op=ALU.add)
        p.emit("vector", red, inc=("sV", 1))
        def sq(eng, t=t):
            return eng.activation(out=xslot, in_=x2slot, func=ACT.Square)
        p.emit("scalar", sq, waits=[("sV", x2d)], inc=("sS", 1))
        sqd = p.counts["sS"]
        def sqr(eng, t=t):
            return eng.tensor_reduce(out=fscr[:, 2:3],
                                     in_=xslot, axis=AX.X, op=ALU.add)
        p.emit("vector", sqr, waits=[("sS", sqd)], inc=("sV", 1))
        emit_stats_tail(p, g, 2 * NT + t, 3 * NT + t, sqd, fscr)
        def t2(eng, t=t):
            return eng.tensor_scalar(out=xslot, in0=x2slot,
                                     scalar1=g["stat"][:, 2 * NT + t:2 * NT + t + 1],
                                     scalar2=g["stat"][:, 3 * NT + t:3 * NT + t + 1],
                                     op0=ALU.subtract, op1=ALU.mult)
        p.emit("vector", t2, waits=[("sV", g["cbsb_done"])] if t == 0 else (),
               inc=("sV", 1))
        def h2a(eng):
            return eng.tensor_tensor(out=x2slot, in0=xslot, in1=g["sgrep"][:, 0:D],
                                     op=ALU.mult)
        p.emit("vector", h2a, waits=[("dS", wpark), ("sS", sqd)], inc=("sV", 1))
        def h2b(eng):
            return eng.tensor_tensor(out=xslot, in0=x2slot, in1=g["srep2"][:, 0:D],
                                     op=ALU.add)
        p.emit("vector", h2b, inc=("sV", 1))
        def amx(eng, t=t):
            return eng.tensor_reduce(out=g["stat"][:, 4 * NT + t:4 * NT + t + 1],
                                     in_=xslot, axis=AX.X, op=ALU.max,
                                     apply_absolute_value=True)
        p.emit("vector", amx, inc=("sV", 1))
        emit_gap(p, g)
        def qs1(eng, t=t):
            return eng.tensor_scalar(out=g["stat"][:, 5 * NT + t:5 * NT + t + 1],
                                     in0=g["stat"][:, 4 * NT + t:4 * NT + t + 1],
                                     scalar1=1e-5, scalar2=None, op0=ALU.max)
        p.emit("vector", qs1, inc=("sV", 1))
        emit_gap(p, g)
        def qs2(eng, t=t):
            return eng.reciprocal(out=fscr[:, 0:1],
                                  in_=g["stat"][:, 5 * NT + t:5 * NT + t + 1])
        p.emit("vector", qs2, inc=("sV", 1))
        emit_gap(p, g)
        def qs3(eng, t=t):
            return eng.tensor_scalar(out=g["stat"][:, 5 * NT + t:5 * NT + t + 1],
                                     in0=fscr[:, 0:1],
                                     scalar1=128.0, scalar2=None, op0=ALU.mult)
        p.emit("vector", qs3, inc=("sV", 1))
        emit_gap(p, g)
        for hf in range(4):
            xs = xslot[:, hf * 512:(hf + 1) * 512]
            def q1(eng, xs=xs, t=t):
                return eng.tensor_scalar(out=aq1, in0=xs,
                                         scalar1=g["stat"][:, 5 * NT + t:5 * NT + t + 1],
                                         scalar2=256.0, op0=ALU.mult, op1=ALU.add)
            p.emit("vector", q1, inc=("sV", 1))
            def q2(eng):
                return eng.tensor_copy(out=aq2, in_=aq1)
            p.emit("vector", q2, inc=("sV", 1))
            def q3(eng):
                return eng.scalar_tensor_tensor(out=aq3, in0=aq2, scalar=0.5, in1=aq1,
                                                op0=ALU.add, op1=ALU.is_le)
            p.emit("vector", q3, inc=("sV", 1))
            def q4(eng):
                return eng.scalar_tensor_tensor(out=aq1, in0=aq2, scalar=-256.0, in1=aq3,
                                                op0=ALU.add, op1=ALU.add)
            p.emit("vector", q4, inc=("sV", 1))
            def q5(eng, t=t, hf=hf):
                return eng.tensor_scalar(
                    out=ARB(g, A_SEG1 + t * 1024, 2048)[:, hf * 512:(hf + 1) * 512],
                    in0=aq1, scalar1=-128.0, scalar2=127.0, op0=ALU.max, op1=ALU.min)
            p.emit("vector", q5, inc=("sV", 1))
        qd = p.counts["sV"]
        for c in range(NI):
            p.dma("scalar",
                  ARB(g, A_SEG2, NI * TOK)[:, c * TOK + t * 128: c * TOK + t * 128 + 128],
                  ARB(g, A_SEG1 + t * 1024, 2048)[:, c * 128:(c + 1) * 128],
                  "dT", waits=[("sV", qd)] if c == 0 else (), transpose=True)
    g["h2q_dT"] = p.counts["dT"]
    g["wo_done_sT"] = p.counts["sT"]
    g["wo_done_sV"] = p.counts["sV"]



def emit_pass1_means(p, g):
    ring = [ARB(g, A_SEG3, 2048), ARB(g, A_SEG3 + 1024, 2048)]
    srcs = [(0, "wgT", D, DI), (1, "wuT", D, DI), (2, "wdS", DI, D)]
    red_done = {}
    si = 0
    for wi, wname, rows, cols in srcs:
        k = 0
        K = (rows // 128) * (cols // 2048)
        def emit_acc(j, wi=wi):
            dst = wi if j % 2 == 0 else 4 + wi
            other = 4 + wi if j % 2 == 0 else wi
            if j == 0:
                p.emit("vector", lambda eng, dst=dst: eng.tensor_copy(
                    out=g["wsum"][:, dst:dst + 1], in_=g["wsum"][:, 8:9]),
                    inc=("sV", 1))
            else:
                p.emit("vector", lambda eng, dst=dst, other=other, j=j:
                       eng.tensor_tensor(
                           out=g["wsum"][:, dst:dst + 1],
                           in0=g["wsum"][:, other:other + 1],
                           in1=g["wsum"][:, 8 + j % 2:9 + j % 2], op=ALU.add),
                       inc=("sV", 1))
        for r in range(rows // 128):
            for cb in range(cols // 2048):
                slab = ring[si % 2]
                wt = [("sV", red_done[si - 2])] if si >= 2 else [("sT", g["wo_done_sT"])]
                w = p.dma("gpsimd", slab,
                          g[wname][r * 128:(r + 1) * 128, cb * 2048:(cb + 1) * 2048],
                          "dG", waits=wt)
                def red(eng, slab=slab, k=k):
                    return eng.tensor_reduce(out=g["wsum"][:, 8 + k % 2:9 + k % 2],
                                             in_=slab, axis=AX.X, op=ALU.add,
                                             apply_absolute_value=True)
                p.emit("vector", red, waits=[("dG", w)], inc=("sV", 1))
                red_done[si] = p.counts["sV"]
                if k >= 1:
                    emit_acc(k - 1)
                k += 1
                si += 1
        emit_gap(p, g)
        emit_acc(K - 1)
        if (K - 1) % 2 == 1:
            emit_gap(p, g)
            p.emit("vector", lambda eng, wi=wi: eng.tensor_copy(
                out=g["wsum"][:, wi:wi + 1], in_=g["wsum"][:, 4 + wi:5 + wi]),
                inc=("sV", 1))
    vfin = p.counts["sV"]
    def csum(eng):
        return eng.matmul(out=g["ps3"][0:1, 512:515], lhsT=g["c_onescf"][:],
                          rhs=g["wsum"][:, 0:3], start=True, stop=True)
    p.emit("tensor", csum, waits=[("sV", vfin)], inc=("sT", 1))
    tcs = p.counts["sT"]
    rowm = g["rows"][0:1, 896:904]
    def swm(eng, rowm=rowm):
        return eng.tensor_scalar(out=rowm[0:1, 0:3], in0=g["ps3"][0:1, 512:515],
                                 scalar1=1.0 / (D * DI), scalar2=1e-5,
                                 op0=ALU.mult, op1=ALU.max)
    p.emit("vector", swm, waits=[("sT", tcs)], inc=("sV", 1))
    emit_gap(p, g)
    def swr(eng):
        return eng.reciprocal(out=g["rows"][0:1, 904:907], in_=g["rows"][0:1, 896:899])
    p.emit("vector", swr, inc=("sV", 1))
    rdone = p.counts["sV"]
    def repm(eng, rowm=rowm):
        return eng.matmul(out=g["ps3"][:, 512:515], lhsT=g["c_onesrf"][:],
                          rhs=rowm[0:1, 0:3], start=True, stop=True)
    p.emit("tensor", repm, waits=[("sV", rdone)], inc=("sT", 1))
    def repw(eng):
        return eng.matmul(out=g["ps2"][:, 512:515], lhsT=g["c_onesrf"][:],
                          rhs=g["rows"][0:1, 904:907], start=True, stop=True)
    p.emit("tensor", repw, inc=("sT", 1))
    trep = p.counts["sT"]
    p.emit("vector", lambda eng: eng.tensor_copy(out=g["wsum"][:, 9:12],
                                                 in_=g["ps3"][:, 512:515]),
           waits=[("sT", trep)], inc=("sV", 1))
    p.emit("vector", lambda eng: eng.tensor_copy(out=g["swcol"][:, 0:3],
                                                 in_=g["ps2"][:, 512:515]),
           inc=("sV", 1))
    g["sw_done"] = p.counts["sV"]





def ffn_chunk(g, a):
    seg = [A_SEG0, A_SEG1, A_SEG3, A_SEG4][a // 16]
    return ARB(g, seg + (a % 16) * 512, TOK)


def emit_ternarize(p, g, s32, dst, sw_ci, idx, extra_wait):
    n = s32.shape[1]
    eng_n = ("vector", "gpsimd")[idx % 2]
    if eng_n == "vector":
        tmpa = ARB(g, A_SEG5 + S5_SHR, 512)
        tmpb = ARB(g, A_SEG5 + S5_SHR + 256, 512)
        def ta(eng):
            return eng.tensor_scalar(out=tmpa[:, 0:n], in0=s32,
                                     scalar1=g["swcol"][:, sw_ci:sw_ci + 1],
                                     scalar2=0.5, op0=ALU.mult, op1=ALU.is_ge)
        p.emit("vector", ta, waits=list(extra_wait) + [("sV", g["sw_done"])],
               inc=("sV", 1))
        def tb(eng):
            return eng.tensor_scalar(out=tmpb[:, 0:n], in0=s32,
                                     scalar1=g["swcol"][:, sw_ci:sw_ci + 1],
                                     scalar2=-0.5, op0=ALU.mult, op1=ALU.is_le)
        p.emit("vector", tb, inc=("sV", 1))
        def tc(eng):
            return eng.tensor_tensor(out=dst, in0=tmpa[:, 0:n], in1=tmpb[:, 0:n],
                                     op=ALU.subtract)
        p.emit("vector", tc, inc=("sV", 1))
        return ("sV", p.counts["sV"])
    else:
        tmpc = g["c_cos"][:, 0:512]
        tmpd = g["c_cos"][:, 512:1024]
        def ga_(eng):
            return eng.tensor_scalar(out=tmpc[:, 0:n], in0=s32,
                                     scalar1=g["swcol"][:, sw_ci:sw_ci + 1],
                                     scalar2=0.5, op0=ALU.mult, op1=ALU.is_ge)
        p.emit("gpsimd", ga_, waits=list(extra_wait) + [("sV", g["sw_done"])] +
               [("sG", g.get("tern_sG", 0))], inc=("sG", 1))
        def gb_(eng):
            return eng.tensor_scalar(out=tmpd[:, 0:n], in0=s32,
                                     scalar1=g["swcol"][:, sw_ci:sw_ci + 1],
                                     scalar2=-0.5, op0=ALU.mult, op1=ALU.is_le)
        p.emit("gpsimd", gb_, inc=("sG", 1))
        def gc_(eng):
            return eng.tensor_tensor(out=dst, in0=tmpc[:, 0:n], in1=tmpd[:, 0:n],
                                     op=ALU.subtract)
        p.emit("gpsimd", gc_, inc=("sG", 1))
        g["tern_sG"] = p.counts["sG"]
        return ("sG", p.counts["sG"])


def emit_gateup(p, g):
    h2qd = g["h2q_dT"]
    rowf = g["rows"][0:1, 768:896]
    for t in range(NT):
        def colb(eng, t=t):
            return eng.tensor_copy(out=g["wsum"][:, 12:13],
                                   in_=g["stat"][:, 4 * NT + t:4 * NT + t + 1])
        p.emit("vector", colb, waits=[("dT", h2qd)] if t == 0 else
               [("sT", p.counts["sT"])], inc=("sV", 1))
        cbd = p.counts["sV"]
        def c2r(eng):
            return eng.matmul(out=g["ps2"][0:1, 512:512 + 128],
                              lhsT=g["wsum"][:, 12:13],
                              rhs=g["c_eyef"][:], start=True, stop=True)
        p.emit("tensor", c2r, waits=[("sV", cbd)], inc=("sT", 1))
        td = p.counts["sT"]
        def rowb(eng, rowf=rowf):
            return eng.tensor_scalar(out=rowf, in0=g["ps2"][0:1, 512:512 + 128],
                                     scalar1=1.0 / 128.0, scalar2=None,
                                     op0=ALU.mult)
        p.emit("vector", rowb, waits=[("sT", td)], inc=("sV", 1))
        rbd = p.counts["sV"]
        def rep(eng, rowf=rowf):
            return eng.matmul(out=g["ps3"][:, 0:128], lhsT=g["c_onesrf"][:],
                              rhs=rowf, start=True, stop=True)
        p.emit("tensor", rep, waits=[("sV", rbd)], inc=("sT", 1))
        tn = p.counts["sT"]
        def repe(eng, t=t):
            return eng.tensor_copy(out=g["sgrep"][:, t * 128:(t + 1) * 128],
                                   in_=g["ps3"][:, 0:128])
        p.emit("vector", repe, waits=[("sT", tn)], inc=("sV", 1))
    sgd = p.counts["sV"]

    h2qf3 = ARB(g, A_SEG2, NI * TOK).rearrange("p (c t) -> p c t", c=NI)
    ring32 = [AR(g, A_SEG5 + S5_FS1, 512), AR(g, A_SEG5 + S5_FS2, 512)]
    tern = [ARB(g, A_SEG5 + S5_FS3, 512), ARB(g, A_SEG5 + S5_FS3 + 256, 512)]
    fscr = AR(g, A_SEG5, 1024)
    fscr2 = AR(g, A_SEG5 + 1024, 1024)
    X3 = ARB(g, A_SEG5 + 2048, 2048)
    mm_of_slab = {}
    si = 0
    for grp in range(16):
        for kindi, wname, swci, mci in ((0, "wgT", 0, 9), (1, "wuT", 1, 10)):
            for i in range(NI):
                slab = ring32[si % 2]
                terns = tern[si % 2]
                wt = [("sT", mm_of_slab[si - 2])] if si >= 2 else [("sV", sgd)]
                w = p.dma("sync", slab,
                          g[wname][i * 128:(i + 1) * 128, grp * 512:(grp + 1) * 512],
                          "dS", waits=wt)
                tdone = emit_ternarize(p, g, slab, terns, swci, si, [("dS", w)])
                for sub in range(4):
                    pq = g[f"ps{sub}"]
                    for half in range(2):
                        wt2 = []
                        if sub == 0 and half == 0:
                            wt2.append(tdone)
                        if i == 0 and sub == 0 and half == 0:
                            wt2.append(("sV", p.counts["sV"]))
                        def mm(eng, terns=terns, sub=sub, i=i, pq=pq, half=half):
                            return eng.matmul(out=pq[:, half * 512:half * 512 + 512],
                                              lhsT=terns[:, sub * 128:(sub + 1) * 128],
                                              rhs=h2qf3[:, i, half * 512:half * 512 + 512],
                                              start=(i == 0), stop=(i == NI - 1))
                        p.emit("tensor", mm, waits=wt2, inc=("sT", 1))
                mm_of_slab[si] = p.counts["sT"]
                si += 1
            kd = p.counts["sT"]
            for sub in range(4):
                a = grp * 4 + sub
                pq = g[f"ps{sub}"]
                if kindi == 0:
                    def e1(eng, pq=pq, mci=mci):
                        return eng.scalar_tensor_tensor(out=fscr, in0=pq[:, 0:TOK],
                                                        scalar=g["wsum"][:, mci:mci + 1],
                                                        in1=g["sgrep"][:, 0:TOK],
                                                        op0=ALU.mult, op1=ALU.mult)
                    p.emit("vector", e1, waits=[("sT", kd), ("sS", p.counts["sS"])],
                           inc=("sV", 1))
                    e1d = p.counts["sV"]
                    def e2(eng, a=a):
                        return eng.activation(out=ffn_chunk(g, a), in_=fscr, func=ACT.Silu)
                    p.emit("scalar", e2, waits=[("sV", e1d)], inc=("sS", 1))
                else:
                    def e3(eng, pq=pq, mci=mci):
                        return eng.scalar_tensor_tensor(out=fscr2, in0=pq[:, 0:TOK],
                                                        scalar=g["wsum"][:, mci:mci + 1],
                                                        in1=g["sgrep"][:, 0:TOK],
                                                        op0=ALU.mult, op1=ALU.mult)
                    p.emit("vector", e3, waits=[("sT", kd), ("sS", p.counts["sS"])],
                           inc=("sV", 1))
                    def e4a(eng, a=a):
                        return eng.tensor_copy(out=X3[:, 0:TOK], in_=ffn_chunk(g, a))
                    p.emit("vector", e4a, inc=("sV", 1))
                    def e4(eng, a=a):
                        return eng.tensor_tensor(out=ffn_chunk(g, a), in0=X3[:, 0:TOK],
                                                 in1=fscr2, op=ALU.mult)
                    p.emit("vector", e4, inc=("sV", 1))
    g["int_done"] = p.counts["sV"]



def emit_fwht(p, g):
    intd = g["int_done"]
    fscr = AR(g, A_SEG5, 1024)
    fscr2 = AR(g, A_SEG5 + 1024, 1024)
    zT_w = {}
    for a in range(NA):
        pq = g[f"ps{a % 4}"]
        for half in range(2):
            wt = []
            if half == 0:
                wt.append(("sV", p.counts["sV"]))
                if a == 0:
                    wt.append(("sV", intd))
            def mm(eng, a=a, half=half, pq=pq):
                return eng.matmul(out=pq[:, half * 512:half * 512 + 512],
                                  lhsT=g["c_h128"][:],
                                  rhs=ffn_chunk(g, a)[:, half * 512:half * 512 + 512],
                                  start=True, stop=True)
            p.emit("tensor", mm, waits=wt, inc=("sT", 1))
        mmd = p.counts["sT"]
        def ev(eng, a=a, pq=pq):
            return eng.tensor_copy(out=ffn_chunk(g, a), in_=pq[:, 0:TOK])
        p.emit("vector", ev, waits=[("sT", mmd)], inc=("sV", 1))
        evd = p.counts["sV"]
        p.dma("sync", g["zpark"][a, :, :], ffn_chunk(g, a), "dS", waits=[("sV", evd)])
    zparkd = p.counts["dS"]
    for c in range(64):
        ap = g["zpark"][:, 2 * c:2 * c + 2, :].rearrange("a b t -> b a t")
        zT_w[c] = p.dma("sync", ffn_chunk(g, c), ap, "dS",
                        waits=[("dS", zparkd)] if c == 0 else ())
    shufd = p.counts["dS"]
    for c in range(64):
        pq = g[f"ps{c % 4}"]
        for half in range(2):
            wt = [("dS", zT_w[c])] if half == 0 else []
            if half == 0:
                wt.append(("sV", p.counts["sV"]))
                if c == 0:
                    wt.append(("dS", shufd))
            def mm(eng, c=c, half=half, pq=pq):
                return eng.matmul(out=pq[:, half * 512:half * 512 + 512],
                                  lhsT=g["c_bd64"][:],
                                  rhs=ffn_chunk(g, c)[:, half * 512:half * 512 + 512],
                                  start=True, stop=True)
            p.emit("tensor", mm, waits=wt, inc=("sT", 1))
        mmd = p.counts["sT"]
        def ev(eng, c=c, pq=pq):
            return eng.tensor_copy(out=ffn_chunk(g, c), in_=pq[:, 0:TOK])
        p.emit("vector", ev, waits=[("sT", mmd)], inc=("sV", 1))
        SCR1 = AR(g, A_SEG5 + 2048, 1024)
        SCR2 = AR(g, A_SEG5 + S5_FS1, 1024)
        def amn(eng, pq=pq):
            return eng.tensor_scalar(out=SCR1, in0=pq[:, 0:TOK], scalar1=-1.0,
                                     scalar2=None, op0=ALU.mult)
        p.emit("vector", amn, inc=("sV", 1))
        def ama(eng, pq=pq):
            return eng.tensor_tensor(out=SCR2, in0=SCR1, in1=pq[:, 0:TOK], op=ALU.max)
        p.emit("vector", ama, inc=("sV", 1))
        if c == 0:
            def am0(eng):
                return eng.tensor_copy(out=fscr, in_=SCR2)
            p.emit("vector", am0, inc=("sV", 1))
        else:
            runsrc = fscr if c % 2 == 1 else fscr2
            rundst = fscr2 if c % 2 == 1 else fscr
            def am1(eng, runsrc=runsrc, rundst=rundst):
                return eng.tensor_tensor(out=rundst, in0=runsrc, in1=SCR2, op=ALU.max)
            p.emit("vector", am1, inc=("sV", 1))
    amd = p.counts["sV"]
    runfin = fscr2
    ymc = g["mcol"]
    CB = ARB(g, A_SEG5 + 2048, 1024)
    TT = ARB(g, A_SEG5 + 2048 + 512, 128)
    p.emit("vector", lambda eng: eng.tensor_copy(out=CB, in_=runfin),
           waits=[("sV", amd)], inc=("sV", 1))
    castd = p.counts["sV"]
    for b8 in range(NT):
        wtd = p.dma("scalar", TT, CB[:, b8 * 128:(b8 + 1) * 128], "dT",
                    waits=[("sV", castd if b8 == 0 else p.counts["sV"])],
                    transpose=True)
        def rd(eng, b8=b8):
            return eng.tensor_reduce(out=ymc[:, b8:b8 + 1], in_=TT,
                                     axis=AX.X, op=ALU.max)
        p.emit("vector", rd, waits=[("dT", wtd)], inc=("sV", 1))
    DOWNC = float(np.float32(1.0 / np.sqrt(DI))) / 128.0
    rowf = g["rows"][0:1, 768:896]
    SCRQ = AR(g, A_SEG5 + S5_FS1, 128)
    for t in range(NT):
        def colb(eng, t=t):
            return eng.tensor_copy(out=g["wsum"][:, 12:13], in_=ymc[:, t:t + 1])
        p.emit("vector", colb, waits=[("sT", p.counts["sT"])], inc=("sV", 1))
        cbd = p.counts["sV"]
        def c2r(eng):
            return eng.matmul(out=g["ps2"][0:1, 512:512 + 128],
                              lhsT=g["wsum"][:, 12:13],
                              rhs=g["c_eyef"][:], start=True, stop=True)
        p.emit("tensor", c2r, waits=[("sV", cbd)], inc=("sT", 1))
        td = p.counts["sT"]
        def rowb(eng, rowf=rowf):
            return eng.tensor_scalar(out=rowf, in0=g["ps2"][0:1, 512:512 + 128],
                                     scalar1=1e-5, scalar2=None, op0=ALU.max)
        p.emit("vector", rowb, waits=[("sT", td)], inc=("sV", 1))
        rbd = p.counts["sV"]
        def rep(eng, rowf=rowf):
            return eng.matmul(out=g["ps3"][:, 0:128], lhsT=g["c_onesrf"][:],
                              rhs=rowf, start=True, stop=True)
        p.emit("tensor", rep, waits=[("sV", rbd)], inc=("sT", 1))
        tn = p.counts["sT"]
        def dsw(eng, t=t):
            return eng.tensor_scalar(out=g["srep2"][:, t * 128:(t + 1) * 128],
                                     in0=g["ps3"][:, 0:128],
                                     scalar1=g["wsum"][:, 11:12], scalar2=DOWNC,
                                     op0=ALU.mult, op1=ALU.mult)
        p.emit("vector", dsw, waits=[("sT", tn)] +
               ([("sV", g["sw_done"])] if t == 0 else []), inc=("sV", 1))
        def rcp(eng):
            return eng.reciprocal(out=SCRQ, in_=g["ps3"][:, 0:128])
        p.emit("vector", rcp, inc=("sV", 1))
        def qsw(eng, t=t):
            return eng.tensor_scalar(out=g["sgrep"][:, t * 128:(t + 1) * 128],
                                     in0=SCRQ, scalar1=128.0, scalar2=None,
                                     op0=ALU.mult)
        p.emit("vector", qsw, inc=("sV", 1))
    repdn = p.counts["sV"]
    fi32 = AR(g, A_SEG5 + 2048, 1024, I32)
    for c in range(64):
        ysl = ffn_chunk(g, c)
        QS = AR(g, A_SEG5 + S5_FS1, 1024)
        def q1a(eng, ysl=ysl):
            return eng.tensor_tensor(out=QS, in0=ysl, in1=g["sgrep"][:, 0:TOK],
                                     op=ALU.mult)
        p.emit("vector", q1a, waits=[("sV", repdn)] if c == 0 else (), inc=("sV", 1))
        def q1b(eng):
            return eng.tensor_scalar(out=fscr, in0=QS, scalar1=256.0, scalar2=None,
                                     op0=ALU.add)
        p.emit("vector", q1b, inc=("sV", 1))
        def q2(eng):
            return eng.tensor_copy(out=fi32, in_=fscr)
        p.emit("vector", q2, inc=("sV", 1))
        def q3(eng):
            return eng.scalar_tensor_tensor(out=fscr2, in0=fi32, scalar=0.5, in1=fscr,
                                            op0=ALU.add, op1=ALU.is_le)
        p.emit("vector", q3, inc=("sV", 1))
        def q4(eng):
            return eng.scalar_tensor_tensor(out=QS, in0=fi32, scalar=-256.0, in1=fscr2,
                                            op0=ALU.add, op1=ALU.add)
        p.emit("vector", q4, inc=("sV", 1))
        def q5(eng, ysl=ysl):
            return eng.tensor_scalar(out=ysl, in0=QS, scalar1=-128.0, scalar2=127.0,
                                     op0=ALU.max, op1=ALU.min)
        p.emit("vector", q5, inc=("sV", 1))
    g["yq_done"] = p.counts["sV"]



def emit_down_out(p, g):
    yqd = g["yq_done"]
    ring32 = [AR(g, A_SEG5 + S5_FS1, 512), AR(g, A_SEG5 + S5_FS2, 512)]
    tern = [ARB(g, A_SEG5 + S5_FS3, 512), ARB(g, A_SEG5 + S5_FS3 + 256, 512)]
    fscr = AR(g, A_SEG5, 1024)
    fscr2 = AR(g, A_SEG5 + 1024, 1024)
    fout = ARB(g, A_SEG2, 4096)
    x2rd = AR(g, A_SEG2 + 2048, 512)
    ostg = AR(g, A_SEG2 + 2560, 512)
    mm_of_slab = {}
    si = 0
    for ds in range(4):
        for c in range(64):
            slab = ring32[si % 2]
            terns = tern[si % 2]
            wt = [("sT", mm_of_slab[si - 2])] if si >= 2 else [("sV", yqd)]
            w = p.dma("sync", slab,
                      g["wdS"][c * 128:(c + 1) * 128, ds * 512:(ds + 1) * 512],
                      "dS", waits=wt)
            tdone = emit_ternarize(p, g, slab, terns, 2, si, [("dS", w)])
            for dc in range(4):
                pq = g[f"ps{dc}"]
                for half in range(2):
                    wt2 = []
                    if dc == 0 and half == 0:
                        wt2.append(tdone)
                    if c == 0 and dc == 0 and half == 0:
                        wt2.append(("sV", p.counts["sV"]))
                    def mm(eng, terns=terns, dc=dc, c=c, pq=pq, half=half):
                        return eng.matmul(out=pq[:, half * 512:half * 512 + 512],
                                          lhsT=terns[:, dc * 128:(dc + 1) * 128],
                                          rhs=ffn_chunk(g, c)[:, half * 512:half * 512 + 512],
                                          start=(c == 0), stop=(c == 63))
                    p.emit("tensor", mm, waits=wt2, inc=("sT", 1))
            mm_of_slab[si] = p.counts["sT"]
            si += 1
        mmd = p.counts["sT"]
        for dc in range(4):
            pq = g[f"ps{dc}"]
            def ep(eng, pq=pq, dc=dc, ds=ds):
                return eng.scalar_tensor_tensor(out=fscr, in0=pq[:, 0:TOK],
                                                scalar=g["dcol"][:, 6 * NI + ds * 4 + dc:6 * NI + ds * 4 + dc + 1],
                                                in1=g["srep2"][:, 0:TOK],
                                                op0=ALU.mult, op1=ALU.mult)
            p.emit("vector", ep, waits=[("sT", mmd)], inc=("sV", 1))
            def epb(eng):
                return eng.tensor_copy(out=ARB(g, A_SEG5 + 2048, 1024), in_=fscr)
            p.emit("vector", epb, waits=[("dT", p.counts["dT"])], inc=("sV", 1))
            epd = p.counts["sV"]
            for t in range(NT):
                p.dma("scalar",
                      fout[:, t * 512 + dc * 128: t * 512 + dc * 128 + 128],
                      ARB(g, A_SEG5 + 2048, 1024)[:, t * 128:(t + 1) * 128],
                      "dT", waits=[("sV", epd)], transpose=True)
        ftd = p.counts["dT"]
        for t in range(NT):
            w = p.dma("sync", x2rd,
                      g["x2park"][t * 128:(t + 1) * 128, ds * 512:(ds + 1) * 512],
                      "dS", waits=[("sV", p.counts["sV"])] if t > 0 else [("dT", ftd)])
            def fin(eng, t=t):
                return eng.tensor_tensor(out=ostg, in0=x2rd,
                                         in1=fout[:, t * 512:(t + 1) * 512], op=ALU.add)
            p.emit("vector", fin, waits=[("dS", w), ("dT", ftd)], inc=("sV", 1))
            find = p.counts["sV"]
            p.dma("sync", g["out"][t * 128:(t + 1) * 128, ds * 512:(ds + 1) * 512],
                  ostg, "dS", waits=[("sV", find)])
    g["out_dS"] = p.counts["dS"]


def emit_final_wait(p, g):
    p.emit("vector", lambda eng: eng.nop(),
           waits=[("dS", p.counts["dS"]), ("dT", p.counts["dT"]),
                  ("dG", p.counts["dG"]), ("sS", p.counts["sS"]),
                  ("sG", p.counts["sG"]), ("sT", p.counts["sT"])])



_NC_CACHE = []


def kernel(L0_attn_hidden_states, L0_attn_Wq, L0_attn_bq, L0_attn_Wk, L0_attn_bk,
           L0_attn_Wv, L0_attn_bv, L0_attn_Wo, L0_attn_bo, L0_attn_rand_matrix,
           L0_ffn_w_gate, L0_ffn_w_up, L0_ffn_w_down, L0_c, L0_wmod):
    f32 = np.float32
    x = np.ascontiguousarray(np.asarray(L0_attn_hidden_states, f32).reshape(B * S, D))
    wqT = np.ascontiguousarray(np.asarray(L0_attn_Wq, f32).T)
    wkT = np.ascontiguousarray(np.asarray(L0_attn_Wk, f32).T)
    wvT = np.ascontiguousarray(np.asarray(L0_attn_Wv, f32).T)
    woT = np.ascontiguousarray(np.asarray(L0_attn_Wo, f32).T)
    wgT = np.ascontiguousarray(np.asarray(L0_ffn_w_gate, f32).T)
    wuT = np.ascontiguousarray(np.asarray(L0_ffn_w_up, f32).T)
    wdS = np.ascontiguousarray(np.asarray(L0_ffn_w_down, f32).T[_down_perm()])
    wmod = np.ascontiguousarray(np.asarray(L0_wmod, f32))
    rT = np.ascontiguousarray(np.asarray(L0_attn_rand_matrix, f32).T)
    cvec = np.asarray(L0_c, f32)
    bqc = np.ascontiguousarray(np.asarray(L0_attn_bq, f32).reshape(NI, 128).T)
    bkc = np.ascontiguousarray(np.asarray(L0_attn_bk, f32).reshape(HKV, 128).T)
    bvc = np.ascontiguousarray(np.asarray(L0_attn_bv, f32).reshape(HKV, 128).T)
    boc = np.ascontiguousarray(np.asarray(L0_attn_bo, f32).reshape(NI, 128).T)
    h128 = _hadamard(128)
    h64 = _hadamard(64)
    bd64 = np.zeros((128, 128), f32)
    bd64[:64, :64] = h64; bd64[64:, 64:] = h64
    pmat = np.zeros((128, 128), f32)
    for i in range(64):
        pmat[i, i + 64] = -1.0
        pmat[i + 64, i] = 1.0
    pT = np.ascontiguousarray(pmat.T)
    onesr = np.ones((1, 128), f32)
    onescol = np.ones((128, 1), f32)
    eye = np.eye(128, dtype=f32)
    cc = np.zeros((128, 8), f32)
    cc[:, 0] = 0.5; cc[:, 1] = 1.0; cc[:, 2] = 1e-5; cc[:, 3] = -0.5; cc[:, 4] = 1.0

    if not _NC_CACHE:
        _NC_CACHE.append(build_nc())
    nc = _NC_CACHE[0]
    in_maps = []
    for cix in range(NCORES):
        b_ = cix // 4
        p4 = cix % 4
        s_rows = np.concatenate([np.arange(256 * (p4 + 4 * q), 256 * (p4 + 4 * q) + 256)
                                 for q in range(4)])
        o_rows = np.concatenate([np.arange(16 * (p4 + 4 * w), 16 * (p4 + 4 * w) + 16)
                                 for w in range(64)])
        pos = s_rows
        cos, sin = _rope_tables(pos)
        in_maps.append({
            "x": np.ascontiguousarray(x[b_ * S:(b_ + 1) * S][s_rows]),
            "xo": np.ascontiguousarray(x[b_ * S:(b_ + 1) * S][o_rows]),
            "wqT": wqT, "wkT": wkT, "wvT": wvT, "woT": woT,
            "wgT": wgT, "wuT": wuT, "wdS": wdS, "wmod": wmod, "rT": rT,
            "cvec": cvec[b_:b_ + 1],
            "bqc": bqc, "bkc": bkc, "bvc": bvc, "boc": boc,
            "h128": h128, "bd64": bd64, "pT": pT, "eye": eye,
            "onesr": onesr, "onescol": onescol,
            "costab": cos.astype(BF), "sintab": sin.astype(BF), "cc": cc,
        })
    res = run_bass_kernel_spmd(nc, in_maps, core_ids=list(range(NCORES)))
    out = np.zeros((B, S, D), np.float32)
    for cix in range(NCORES):
        b_ = cix // 4
        p4 = cix % 4
        o_rows = np.concatenate([np.arange(16 * (p4 + 4 * w), 16 * (p4 + 4 * w) + 16)
                                 for w in range(64)])
        out[b_, o_rows] = res.results[cix]["out"]
    return out
